# version 04
# baseline (speedup 1.0000x reference)
import numpy as np
import ml_dtypes
import concourse.bass as bass
import concourse.mybir as mybir
from concourse.bass_utils import run_bass_kernel_spmd

F32 = mybir.dt.float32
BF16 = mybir.dt.bfloat16
AF = mybir.ActivationFunctionType
ALU = mybir.AluOpType
AX = mybir.AxisListType.X

T = 2048
C = 2048
NT = 16
NH = 16
NG = 4
KV = 512
SCALE = 1.0 / np.sqrt(128.0)
NEG = -1.0e30
EPS = 1e-5
KB = 1024

ENGS = ("sync", "gpsimd", "scalar", "vector", "tensor")


class Prog:
    def __init__(self, nc):
        self.nc = nc
        self.sems = {}
        self.counts = {}
        self.ops = {e: [] for e in ENGS}

    def sem(self, name):
        self.sems[name] = self.nc.alloc_semaphore(name)
        self.counts[name] = 0

    def add(self, eng, fn, deps=(), inc=None, inck=1):
        ev = None
        if inc is not None:
            self.counts[inc] += inck
            ev = (inc, self.counts[inc])
        flat = []
        for d in deps:
            if d is None:
                continue
            if isinstance(d, list):
                flat += [x for x in d if x is not None]
            else:
                flat.append(d)
        self.ops[eng].append((fn, tuple(flat), inc, inck))
        return ev

    def dma(self, eng, out_ap, in_ap, deps=(), **kw):
        inc = {"sync": "ldx", "gpsimd": "ldw", "scalar": "st"}[eng]
        def fn(e, out_ap=out_ap, in_ap=in_ap, kw=kw):
            return e.dma_start(out_ap, in_ap, **kw)
        return self.add(eng, fn, deps, inc=inc, inck=16)

    def emit(self, block):
        def mk(name):
            def run(eng):
                hw = {}
                for fn, deps, inc, inck in self.ops[name]:
                    for (s, v) in deps:
                        if hw.get(s, 0) < v:
                            eng.wait_ge(self.sems[s], v)
                            hw[s] = v
                    instr = fn(eng)
                    if inc is not None:
                        instr.then_inc(self.sems[inc], inck)
            return run
        block.sync(mk("sync"))
        block.gpsimd(mk("gpsimd"))
        block.scalar(mk("scalar"))
        block.vector(mk("vector"))
        block.tensor(mk("tensor"))


def _rope_tables():
    invf = 1.0 / (10000.0 ** (np.arange(64, dtype=np.float64) * 2.0 / 128.0))
    ang = np.arange(T, dtype=np.float64)[None, :] * invf[:, None]
    cos = np.cos(ang).astype(np.float32)
    sin = np.sin(ang).astype(np.float32)
    return (np.concatenate([cos, cos], 0), np.concatenate([-sin, sin], 0))


def _masks():
    i = np.arange(128)[:, None]
    j = np.arange(128)[None, :]
    m2 = np.where(i >= j, 0.0, NEG).astype(ml_dtypes.bfloat16)
    m1 = np.where(j >= i, 0.0, NEG).astype(ml_dtypes.bfloat16)
    m1s = np.where((j >= i) | (j < 4), 0.0, NEG).astype(ml_dtypes.bfloat16)
    return m1, m2, m1s


def build_nc(debug=False):
    nc = bass.Bass("TRN2", target_bir_lowering=False)

    x_h = nc.dram_tensor("x", [T, C], F32, kind="ExternalInput")
    wq_h = nc.dram_tensor("wq", [C, C], F32, kind="ExternalInput")
    bq_h = nc.dram_tensor("bq", [NH, 128], F32, kind="ExternalInput")
    wk_h = nc.dram_tensor("wk", [C, KV], F32, kind="ExternalInput")
    bk_h = nc.dram_tensor("bk", [NG, 128], F32, kind="ExternalInput")
    wv_h = nc.dram_tensor("wv", [C, KV], F32, kind="ExternalInput")
    bv_h = nc.dram_tensor("bv", [1, KV], F32, kind="ExternalInput")
    wo_h = nc.dram_tensor("wo", [C, C], F32, kind="ExternalInput")
    bo_h = nc.dram_tensor("bo", [1, C], F32, kind="ExternalInput")
    wf_h = nc.dram_tensor("wf", [C, C], F32, kind="ExternalInput")
    bf_h = nc.dram_tensor("bf", [1, C], F32, kind="ExternalInput")
    out_h = nc.dram_tensor("out", [T, C], F32, kind="ExternalOutput")
    dk = "ExternalOutput" if debug else "Internal"
    x1_h = nc.dram_tensor("x1s", [T, C], F32, kind=dk)
    x2_h = nc.dram_tensor("x2s", [T, C], F32, kind=dk)
    bwT_h = nc.dram_tensor("bwTs", [128, 16 * C], BF16, kind=dk)
    if debug:
        qt_h = nc.dram_tensor("qts", [128, NH * T], BF16, kind="ExternalOutput")
        kt_h = nc.dram_tensor("kts", [128, NG * T], BF16, kind="ExternalOutput")
        v_h = nc.dram_tensor("vs", [128, NT * KV], BF16, kind="ExternalOutput")
        ao_h = nc.dram_tensor("aos", [128, NH * T], BF16, kind="ExternalOutput")
        ln2_h = nc.dram_tensor("ln2s", [128, 16 * T], BF16, kind="ExternalOutput")

    cc2_np, ss2_np = _rope_tables()
    m1_np, m2_np, m1s_np = _masks()
    cc2_h = nc.inline_tensor(cc2_np, name="cc2")
    ss2_h = nc.inline_tensor(ss2_np, name="ss2")
    m1_h = nc.inline_tensor(m1_np, name="m1")
    m2_h = nc.inline_tensor(m2_np, name="m2")
    m1s_h = nc.inline_tensor(m1s_np, name="m1s")
    id_h = nc.inline_tensor(np.eye(128, dtype=ml_dtypes.bfloat16), name="id128")
    ones_h = nc.inline_tensor(np.ones((1, 512), dtype=ml_dtypes.bfloat16), name="ones1")
    vca_np = np.tile(np.array([[0.5, 0.5 / 2048.0, -1.0 / 2048.0, 1.0 / 2048.0,
                                -0.5, SCALE, 0.0, 0.0]], dtype=np.float32), (128, 1))
    vcb_np = np.tile(np.array([[EPS, 1.5]], dtype=np.float32), (128, 1))
    vca_h = nc.inline_tensor(vca_np, name="vca")
    vcb_h = nc.inline_tensor(vcb_np, name="vcb")

    def at(name, shape, dtype, off, align=32):
        return nc.alloc_sbuf_tensor_at(name, shape, dtype, offset=off,
                                       align_bytes=align)

    o = 17024
    id_sb = at("id_sb", [128, 128], BF16, o); o += 256
    m1_sb = at("m1_sb", [128, 128], BF16, o); o += 256
    m2_sb = at("m2_sb", [128, 128], BF16, o); o += 256

    vca = at("vca_sb", [128, 8], F32, o); o += 32
    vcb = at("vcb_sb", [128, 2], F32, o); o += 8
    SC = [[at(f"st{p}_{c}", [128, 1], F32, o + (p * 16 + c) * 16, align=8)
           for c in range(13)] for p in range(2)]
    al0 = at("al0", [128, 1], F32, o + 28 * 16, align=8)
    al1 = at("al1", [128, 1], F32, o + 29 * 16, align=8)
    o += 30 * 16
    o = (o + 31) // 32 * 32
    bff_bf = at("bff_bf", [1, C], BF16, o); o += 4 * KB
    X = (o + 63) // 64 * 64
    lnT2 = at("lnT2", [128, 16 * T], BF16, X)
    wo_all = at("wo_all", [128, 16 * C], BF16, X)
    bwT_e = at("bwT_e", [128, 16 * C], BF16, X)
    Y = X + 64 * KB
    bwT = at("bwT", [128, 16 * C], BF16, Y)
    QT = at("QT", [128, NH * T], BF16, Y)
    KTV = Y + 64 * KB
    KT = at("KT", [128, NG * T], BF16, KTV)
    v_sb = at("v_sb", [128, NT * KV], BF16, KTV + 16 * KB)
    Z = KTV + 32 * KB
    xb0 = at("xb0", [128, C], F32, Z)
    xb1 = at("xb1", [128, C], F32, Z + 8 * KB)
    ln_bf = at("ln_bf", [128, C], BF16, Z + 16 * KB)
    lnT1 = at("lnT1", [128, C], BF16, Z + 20 * KB)
    wqg = at("wqg", [128, 16 * 512], BF16, Z)
    cc2_sb = at("cc2_sb", [128, T], F32, KTV + 16 * KB)
    ss2_sb = at("ss2_sb", [128, T], F32, KTV + 24 * KB)
    P1 = [at(f"P1_{p}", [128, 512], BF16, Z + p * 4 * KB) for p in range(2)]
    P2 = [at(f"P2_{p}", [128, 132], BF16, Z + p * 4 * KB + 1024) for p in range(2)]
    PTb = [at(f"PT_{p}", [128, 640], BF16, Z + p * 4 * KB + 1408) for p in range(2)]
    PTk = [at(f"PTk_{p}", [4, 128], BF16, Z + p * 4 * KB + 2688) for p in range(2)]
    diag = [at(f"diag_{p}", [128, 128], BF16, Z + p * 4 * KB + 2944) for p in range(2)]
    m1s_sb = at("m1s_sb", [128, 128], BF16, Z + 24 * KB)
    cnames = ("negmax", "negmsc", "acc1", "acc2", "den", "rec")
    cst = [{n: at(f"c_{n}_{p}", [128, 1], F32, Z + p * 4 * KB + 3232 + ci * 32)
            for ci, n in enumerate(cnames)} for p in range(2)]
    bo_bf = at("bo_bf", [1, C], BF16, Z + 16 * KB)
    W = Z + 24 * KB
    wfb0 = at("wfb0", [128, C], F32, W)
    wfb1 = at("wfb1", [128, C], F32, W + 8 * KB)
    sgn_bf = at("sgn_bf", [128, C], BF16, Z + 16 * KB)
    bkr = at("bkr", [1, KV], BF16, W + 16 * KB)
    ones_sb = at("ones_sb", [1, 512], BF16, W + 16 * KB + 1024)
    bf_f32 = at("bf_f32", [1, C], F32, W)
    Qr = at("Qr", [128, T], F32, W)
    Qs = at("Qs", [128, T], F32, W + 8 * KB)
    wkg = at("wkg", [128, 16 * 128], BF16, Z + 16 * KB)
    bqr = at("bqr", [1, C], BF16, Z + 20 * KB)
    bv_bf = at("bv_bf", [1, KV], BF16, Z + 20 * KB)
    wv_all = at("wv_all", [128, 16 * 512], BF16, W)
    total = W + 16 * KB + 2048
    assert total <= 224 * KB - 32, total

    pb = [nc.alloc_psum_tensor(f"pb{i}", [128, 512], F32) for i in range(8)]
    bf_ev = [None] * 8

    P = Prog(nc)
    for s in ("ldx", "ldw", "st", "te", "ve", "se"):
        P.sem(s)

    ev_id = P.dma("gpsimd", id_sb[:], id_h[:])
    ev_vc = P.dma("gpsimd", vca[:], vca_h[:])
    ev_vc2 = P.dma("gpsimd", vcb[:], vcb_h[:])
    ev_m1 = P.dma("gpsimd", m1_sb[:], m1_h[:])
    ev_m2 = P.dma("gpsimd", m2_sb[:], m2_h[:])

    ev_ones = P.dma("gpsimd", ones_sb[:], ones_h[:])
    ev_bk2 = P.dma("gpsimd", bkr[:], bk_h.reshape([1, KV])[:])
    ev_bf32 = P.dma("gpsimd", bf_f32[:], bf_h[:])
    ev_bff = P.add("vector", lambda e: e.tensor_scalar_mul(bff_bf[:], bf_f32[:],
                                                            vca[0:1, 0:1]),
                   deps=(ev_bf32, ev_vc), inc="ve")

    ev_wf_free = [[], []]
    ev_sgn_free = None
    bwT_v = bwT.rearrange("p (i o) -> p i o", i=16)
    for io in range(16):
        b = io % 2
        wf = (wfb0, wfb1)[b]
        dep0 = (ev_bff,) if io == 0 else ()
        evl = P.dma("sync", wf[:], wf_h[io * 128:(io + 1) * 128, :],
                    deps=tuple(ev_wf_free[b]) + dep0)
        eva = P.add("vector",
                    lambda e, wf=wf: e.reduce_sum(al0[:], wf[:], axis=AX,
                                                  apply_absolute_value=True),
                    deps=(evl,), inc="ve")
        eva2 = P.add("vector",
                     lambda e: e.tensor_scalar_mul(al1[:], al0[:],
                                                   vca[:, 1:2]),
                     deps=(eva, ev_vc), inc="ve")
        evs = P.add("scalar",
                    lambda e, wf=wf: e.activation(sgn_bf[:], wf[:], AF.Sign),
                    deps=(evl, ev_sgn_free), inc="se")
        ev_wf_free[b] = [evs, eva]
        evb = P.add("vector",
                    lambda e: e.tensor_scalar_mul(sgn_bf[:], sgn_bf[:], al1[:]),
                    deps=(evs, eva2), inc="ve")
        for gj in range(4):
            k = gj % 2
            mm = None
            for j in range(gj * 4, gj * 4 + 4):
                mm = P.add("tensor",
                           lambda e, j=j, k=k: e.matmul(
                               pb[k][:, (j % 4) * 128:(j % 4) * 128 + 128],
                               sgn_bf[:, j * 128:(j + 1) * 128], id_sb[:],
                               start=True, stop=True),
                           deps=(evb, ev_id, bf_ev[k]), inc="te")
            ev_sgn_free = mm
            eve = P.add("vector",
                        lambda e, gj=gj, io=io, k=k: e.tensor_copy(
                            bwT_v[:, gj * 4:gj * 4 + 4, io * 128:io * 128 + 128],
                            pb[k].rearrange("p (a q) -> p a q", a=4)),
                        deps=(mm,), inc="ve")
            bf_ev[k] = eve
    ev_bw_all = bf_ev[0]
    ev_bw_all2 = bf_ev[1]
    ev_bwT_spill = P.dma("scalar", bwT_h[:], bwT[:], deps=(ev_bw_all, ev_bw_all2))

    ln_hist = [None, None]

    def emit_ln(xb, out_bf, dep_ready, lnbf_free, ncall=[0]):
        p = ncall[0] % 2
        ncall[0] += 1
        s = SC[p]
        stale = ln_hist[p]
        ev_m = P.add("vector", lambda e, xb=xb, s=s: e.reduce_sum(
            s[0][:], xb[:], axis=AX), deps=(dep_ready, stale), inc="ve")
        ev_nm = P.add("vector", lambda e, s=s: e.tensor_scalar_mul(
            s[1][:], s[0][:], vca[:, 2:3]), deps=(ev_m,), inc="ve")
        ev_sq = P.add("scalar", lambda e, xb=xb, s=s, out_bf=out_bf: e.activation(
            out_bf[:], xb[:], AF.Square, accum_out=s[2][:]),
            deps=(dep_ready, stale, lnbf_free), inc="se")
        ev_v1 = P.add("vector", lambda e, s=s: e.tensor_scalar(
            s[3][:], s[2][:], vca[:, 3:4], vcb[:, 0:1], ALU.mult, ALU.add),
            deps=(ev_sq,), inc="ve")
        ev_m2e = P.add("vector", lambda e, s=s: e.tensor_mul(
            s[4][:], s[1][:], s[1][:]), deps=(ev_nm,), inc="ve")
        ev_vp = P.add("vector", lambda e, s=s: e.tensor_sub(
            s[5][:], s[3][:], s[4][:]), deps=(ev_v1, ev_m2e), inc="ve")
        ev_s = P.add("scalar", lambda e, s=s: e.activation(
            s[6][:], s[5][:], AF.Sqrt), deps=(ev_vp,), inc="se")
        ev_r0 = P.add("vector", lambda e, s=s: e.reciprocal(
            s[7][:], s[6][:]), deps=(ev_s,), inc="ve")
        ev_t1 = P.add("vector", lambda e, s=s: e.tensor_mul(
            s[8][:], s[7][:], s[7][:]), deps=(ev_r0,), inc="ve")
        ev_t2 = P.add("vector", lambda e, s=s: e.tensor_mul(
            s[9][:], s[8][:], s[5][:]), deps=(ev_t1,), inc="ve")
        ev_t3 = P.add("vector", lambda e, s=s: e.tensor_scalar(
            s[10][:], s[9][:], vca[:, 4:5], vcb[:, 1:2], ALU.mult, ALU.add),
            deps=(ev_t2,), inc="ve")
        ev_r1 = P.add("vector", lambda e, s=s: e.tensor_mul(
            s[11][:], s[7][:], s[10][:]), deps=(ev_t3,), inc="ve")
        ev_nmr = P.add("vector", lambda e, s=s: e.tensor_mul(
            s[12][:], s[1][:], s[11][:]), deps=(ev_r1,), inc="ve")
        ev_ln = P.add("scalar", lambda e, xb=xb, s=s, out_bf=out_bf: e.activation(
            out_bf[:], xb[:], AF.Identity, bias=s[12][:], scale=s[11][:]),
            deps=(ev_nmr,), inc="se")
        ln_hist[p] = ev_ln
        return ev_ln

    def emit_transpose(src_bf, dest_view, ev_src, dest_cols=None):
        last_e = None
        last_mm = None
        for gj in range(4):
            k = gj % 2
            mm = None
            for j in range(gj * 4, gj * 4 + 4):
                mm = P.add("tensor",
                           lambda e, j=j, k=k, src_bf=src_bf: e.matmul(
                               pb[k][:, (j % 4) * 128:(j % 4) * 128 + 128],
                               src_bf[:, j * 128:(j + 1) * 128], id_sb[:],
                               start=True, stop=True),
                           deps=(ev_src, bf_ev[k]), inc="te")
            last_mm = mm
            if dest_cols is None:
                dv = dest_view[:, gj * 4:gj * 4 + 4, :]
            else:
                dv = dest_view[:, gj * 4:gj * 4 + 4, dest_cols[0]:dest_cols[1]]
            eve = P.add("vector",
                        lambda e, dv=dv, k=k: e.tensor_copy(
                            dv, pb[k].rearrange("p (a q) -> p a q", a=4)),
                        deps=(mm,), inc="ve")
            bf_ev[k] = eve
            last_e = eve
        return last_e, last_mm

    def emit_ffn(lnT_src, wsrc, brow, xb, ev_lnT, ev_ws, ev_x):
        evs = []
        for oc in range(4):
            k = 2 + oc % 2
            mm = None
            for i in range(16):
                mm = P.add("tensor",
                           lambda e, i=i, oc=oc, k=k, lnT_src=lnT_src, wsrc=wsrc:
                           e.matmul(pb[k][:], lnT_src[:, i * 128:(i + 1) * 128],
                                    wsrc[:, i * C + oc * 512:i * C + oc * 512 + 512],
                                    start=(i == 0), stop=False),
                           deps=(ev_lnT, bf_ev[k]) + ev_ws, inc="te")
            mm = P.add("tensor",
                       lambda e, oc=oc, k=k, brow=brow: e.matmul(
                           pb[k][:], ones_sb[:, 0:128], brow[:, oc * 512:oc * 512 + 512],
                           start=False, stop=True),
                       deps=(ev_ones, ev_bff), inc="te")
            eve = P.add("vector",
                        lambda e, oc=oc, k=k, xb=xb: e.tensor_add(
                            xb[:, oc * 512:oc * 512 + 512], pb[k][:],
                            xb[:, oc * 512:oc * 512 + 512]),
                        deps=(mm, ev_x), inc="ve")
            bf_ev[k] = eve
            evs.append(eve)
        return evs

    lnT2_v = lnT2.rearrange("p (i t) -> p i t", i=16)
    lnT1_v = lnT1.rearrange("p (i t) -> p i t", i=16)
    ev_xfree = [(), ()]
    ev_x1sp = []
    ev_lnbf_free = ev_sgn_free
    ev_lnT2_done = []
    last_ln_A = None
    for tt in range(NT):
        b = tt % 2
        xb = (xb0, xb1)[b]
        evx = P.dma("sync", xb[:], x_h[tt * 128:(tt + 1) * 128, :],
                    deps=ev_xfree[b])
        ev_ln1 = emit_ln(xb, ln_bf, evx, ev_lnbf_free)
        ev_t1, t1mm = emit_transpose(ln_bf, lnT1_v, ev_ln1)
        ev_lnbf_free = t1mm
        evf = emit_ffn(lnT1, bwT, bff_bf, xb,
                       ev_t1, (ev_bw_all, ev_bw_all2), evx)
        ev_ln2 = emit_ln(xb, ln_bf, evf[-1], ev_lnbf_free)
        last_ln_A = ev_ln2
        ev_t2, t2mm = emit_transpose(ln_bf, lnT2_v, ev_ln2,
                                     dest_cols=(tt * 128, tt * 128 + 128))
        ev_lnbf_free = t2mm
        ev_lnT2_done.append(ev_t2)
        evsp = P.dma("scalar", x1_h[tt * 128:(tt + 1) * 128, :], xb[:],
                     deps=(ev_ln2,))
        ev_x1sp.append(evsp)
        ev_xfree[b] = (evsp,)
    ev_lnT2_all = ev_lnT2_done[-1]
    ev_A_te = ev_lnbf_free

    ev_cc = P.dma("gpsimd", cc2_sb[:], cc2_h[:], deps=(ev_A_te, last_ln_A))
    ev_ss = P.dma("gpsimd", ss2_sb[:], ss2_h[:])
    ev_bq2 = P.dma("gpsimd", bqr[:], bq_h.reshape([1, C])[:])

    ev_qt = {}
    ev_kt = {}
    ev_qr_free = None
    ev_y_ok = (ev_bwT_spill, ev_A_te, ev_lnT2_all)

    def proj_rope(wbuf, wcols, wcol, brow, bcol, dest, dcol, ev_w):
        nonlocal ev_qr_free
        ev_ev = []
        last_mm = None
        for tc in range(4):
            k = 2 + tc % 2
            mm = None
            for i in range(16):
                mm = P.add("tensor",
                           lambda e, i=i, tc=tc, k=k, wbuf=wbuf, wcols=wcols,
                           wcol=wcol: e.matmul(
                               pb[k][:],
                               wbuf[:, i * wcols + wcol * 128:
                                    i * wcols + wcol * 128 + 128],
                               lnT2[:, i * T + tc * 512:i * T + tc * 512 + 512],
                               start=(i == 0), stop=False),
                           deps=(ev_w, bf_ev[k], ev_lnT2_all), inc="te")
            mm = P.add("tensor",
                       lambda e, k=k, brow=brow, bcol=bcol: e.matmul(
                           pb[k][:], brow[:, bcol * 128:bcol * 128 + 128],
                           ones_sb[:, 0:512], start=False, stop=True),
                       deps=(ev_ones, ev_bq2, ev_bk2), inc="te")
            last_mm = mm
            eve = P.add("scalar",
                        lambda e, tc=tc, k=k: e.copy(
                            Qr[:, tc * 512:tc * 512 + 512], pb[k][:]),
                        deps=(mm, ev_qr_free), inc="se")
            bf_ev[k] = eve
            ev_ev.append(eve)
        evs1 = P.dma("scalar", Qs[0:64, :], Qr[64:128, :], deps=(ev_ev[-1],))
        evs2 = P.dma("scalar", Qs[64:128, :], Qr[0:64, :])
        evu = P.add("vector", lambda e: e.tensor_mul(Qr[:], Qr[:], cc2_sb[:]),
                    deps=(evs2, ev_cc), inc="ve")
        evv = P.add("vector", lambda e: e.tensor_mul(Qs[:], Qs[:], ss2_sb[:]),
                    deps=(evs2, ev_ss), inc="ve")
        evq = P.add("vector",
                    lambda e, dest=dest, dcol=dcol: e.tensor_add(
                        dest[:, dcol * T:(dcol + 1) * T], Qr[:], Qs[:]),
                    deps=(evu, evv) + ev_y_ok, inc="ve")
        ev_qr_free = evq
        return evq, last_mm

    wq_v = wq_h.rearrange("(i p) q -> p i q", p=128)
    wk_v = wk_h.rearrange("(i p) q -> p i q", p=128)
    wv_v = wv_h.rearrange("(i p) q -> p i q", p=128)
    ev_wq_free = None
    for g in range(4):
        dep_wq = (ev_x1sp[14], ev_x1sp[15]) if g == 0 else (ev_wq_free,)
        ev_wq = P.dma("gpsimd", wqg.rearrange("p (i f) -> p i f", i=16),
                      wq_v[:, :, g * 512:(g + 1) * 512], deps=dep_wq)
        dep_wk = (last_ln_A, ev_A_te) if g == 0 else (ev_kt[g - 1],)
        ev_wk = P.dma("gpsimd", wkg.rearrange("p (i f) -> p i f", i=16),
                      wk_v[:, :, g * 128:(g + 1) * 128], deps=dep_wk)
        last_q_mm = None
        for hh in range(4):
            h = g * 4 + hh
            ev_qt[h], last_q_mm = proj_rope(wqg, 512, hh, bqr, h, QT, h, ev_wq)
        ev_wq_free = last_q_mm
        ev_kt[g], _ = proj_rope(wkg, 128, 0, bkr, g, KT, g, ev_wk)

    ev_wv = P.dma("gpsimd", wv_all.rearrange("p (i f) -> p i f", i=16),
                  wv_v[:, :, :], deps=(ev_qt[15], ev_kt[3]))
    ev_bv = P.dma("gpsimd", bv_bf[:], bv_h[:])
    ev_v = []
    for tt in range(NT):
        k = 2 + tt % 2
        mm = None
        for i in range(16):
            mm = P.add("tensor",
                       lambda e, i=i, tt=tt, k=k: e.matmul(
                           pb[k][:], lnT2[:, i * T + tt * 128:i * T + tt * 128 + 128],
                           wv_all[:, i * 512:(i + 1) * 512],
                           start=(i == 0), stop=False),
                       deps=(ev_wv, bf_ev[k], ev_lnT2_all), inc="te")
        mm = P.add("tensor",
                   lambda e, k=k: e.matmul(pb[k][:], ones_sb[:, 0:128], bv_bf[:],
                                           start=False, stop=True),
                   deps=(ev_ones, ev_bv), inc="te")
        eve = P.add("vector",
                    lambda e, tt=tt, k=k: e.tensor_copy(
                        v_sb[:, tt * 512:(tt + 1) * 512], pb[k][:]),
                    deps=(mm,), inc="ve")
        bf_ev[k] = eve
        ev_v.append(eve)

    ev_m1s = P.dma("gpsimd", m1s_sb[:], m1s_h[:], deps=(ev_v[-1],))

    ev_dumpq = None
    if debug:
        ev_dumpq = P.dma("gpsimd", qt_h[:], QT[:], deps=(ev_qt[15],))
        P.dma("gpsimd", kt_h[:], KT[:], deps=(ev_kt[3],))
        P.dma("gpsimd", v_h[:], v_sb[:], deps=(ev_v[-1],))
        P.dma("gpsimd", ln2_h[:], lnT2[:], deps=(ev_lnT2_all,))

    ev_wo = []
    for hh in range(16):
        ev_wo.append(P.dma("gpsimd", wo_all[:, hh * C:(hh + 1) * C],
                           wo_h[hh * 128:(hh + 1) * 128, :],
                           deps=(ev_v[-1],) if hh == 0 else ()))

    evP_free = [None, None]
    evPT_free = [None, None]
    ev_ao = {}
    it = 0
    first_ao = True
    for h in range(NH):
        g = h // 4
        for qt in range(NT):
            p = it % 2
            t0 = qt * 128
            lo = max(0, t0 - 512)
            w1 = t0 - lo
            ns = w1 // 128 + 1
            sink = lo >= 128
            ws2 = 132 if sink else 128
            kS1, kS2, kT = p, 2 + p, 4 + p
            qs = (h * T + t0, h * T + t0 + 128)
            evs1 = None
            if w1 > 0:
                evs1 = P.add("tensor",
                             lambda e, kS1=kS1, qs=qs, g=g, lo=lo, w1=w1: e.matmul(
                                 pb[kS1][:, 0:w1], QT[:, qs[0]:qs[1]],
                                 KT[:, g * T + lo:g * T + lo + w1],
                                 start=True, stop=True),
                             deps=(ev_qt[h], ev_kt[g], bf_ev[kS1]), inc="te")
            evs2 = P.add("tensor",
                         lambda e, kS2=kS2, qs=qs, g=g, t0=t0: e.matmul(
                             pb[kS2][:, 0:128], QT[:, qs[0]:qs[1]],
                             KT[:, g * T + t0:g * T + t0 + 128],
                             start=True, stop=True),
                         deps=(ev_qt[h], ev_kt[g], bf_ev[kS2]), inc="te")
            if sink:
                evs2 = P.add("tensor",
                             lambda e, kS2=kS2, qs=qs, g=g: e.matmul(
                                 pb[kS2][:, 128:132], QT[:, qs[0]:qs[1]],
                                 KT[:, g * T:g * T + 4],
                                 start=True, stop=True),
                             deps=(), inc="te")
            evm2e = P.add("vector",
                          lambda e, kS2=kS2: e.tensor_add(
                              pb[kS2][:, 0:128], pb[kS2][:, 0:128], m2_sb[:]),
                          deps=(evs2, ev_m2), inc="ve")
            evm1e = evs1
            if w1 == 512:
                msk = m1s_sb if qt == 4 else m1_sb
                evm1e = P.add("vector",
                              lambda e, kS1=kS1, msk=msk: e.tensor_add(
                                  pb[kS1][:, 0:128], pb[kS1][:, 0:128], msk[:]),
                              deps=(evs1, ev_m1, ev_m1s), inc="ve")
            evmx = P.add("vector",
                         lambda e, kS2=kS2, ws2=ws2, p=p: e.reduce_max(
                             cst[p]["negmax"][:], pb[kS2][:, 0:ws2], axis=AX, negate=True),
                         deps=(evm2e, evP_free[p]), inc="ve")
            evms = P.add("vector",
                         lambda e, p=p: e.tensor_scalar_mul(
                             cst[p]["negmsc"][:], cst[p]["negmax"][:], vca[:, 5:6]),
                         deps=(evmx,), inc="ve")
            eve1 = None
            if w1 > 0:
                eve1 = P.add("scalar",
                             lambda e, kS1=kS1, p=p, w1=w1: e.activation(
                                 P1[p][:, 0:w1], pb[kS1][:, 0:w1], AF.Exp,
                                 bias=cst[p]["negmsc"][:], scale=SCALE,
                                 accum_out=cst[p]["acc1"][:]),
                             deps=(evm1e, evms), inc="se")
                bf_ev[kS1] = eve1
            eve2 = P.add("scalar",
                         lambda e, kS2=kS2, p=p, ws2=ws2: e.activation(
                             P2[p][:, 0:ws2], pb[kS2][:, 0:ws2], AF.Exp,
                             bias=cst[p]["negmsc"][:], scale=SCALE,
                             accum_out=cst[p]["acc2"][:]),
                         deps=(evms,), inc="se")
            bf_ev[kS2] = eve2
            if w1 > 0:
                evd = P.add("vector",
                            lambda e, p=p: e.tensor_add(
                                cst[p]["den"][:], cst[p]["acc1"][:], cst[p]["acc2"][:]),
                            deps=(eve1, eve2), inc="ve")
            else:
                evd = P.add("vector",
                            lambda e, p=p: e.tensor_copy(cst[p]["den"][:],
                                                         cst[p]["acc2"][:]),
                            deps=(eve2,), inc="ve")
            evr = P.add("vector",
                        lambda e, p=p: e.reciprocal(cst[p]["rec"][:], cst[p]["den"][:]),
                        deps=(evd,), inc="ve")
            evdg = P.add("vector",
                         lambda e, p=p: e.tensor_scalar_mul(
                             diag[p][:], id_sb[:], cst[p]["rec"][:]),
                         deps=(evr, ev_id), inc="ve")
            ptev = []
            for kk in range(ns):
                if kk < w1 // 128:
                    src = P1[p][:, kk * 128:(kk + 1) * 128]
                    dep = eve1
                else:
                    src = P2[p][:, 0:128]
                    dep = eve2
                if kk < 4:
                    dst = pb[kT][:, kk * 128:(kk + 1) * 128]
                    fr = bf_ev[kT] if kk == 0 else None
                else:
                    dst = pb[6][:, 0:128]
                    fr = bf_ev[6]
                ptev.append(P.add("tensor",
                                  lambda e, src=src, dst=dst, p=p: e.matmul(
                                      dst, src, diag[p][:], start=True, stop=True),
                                  deps=(dep, evdg, fr, evPT_free[p]), inc="te"))
            if sink:
                ptev.append(P.add("tensor",
                                  lambda e, p=p: e.matmul(
                                      pb[6][0:4, 128:256], P2[p][:, 128:132],
                                      diag[p][:], start=True, stop=True),
                                  deps=(eve2, evdg, bf_ev[6]), inc="te"))
            evP_free[p] = ptev[-1]
            n4 = min(ns, 4)
            evpt = P.add("vector",
                         lambda e, p=p, n4=n4, kT=kT: e.tensor_copy(
                             PTb[p][:, 0:n4 * 128], pb[kT][:, 0:n4 * 128]),
                         deps=(ptev[n4 - 1],), inc="ve")
            bf_ev[kT] = evpt
            if ns == 5:
                evpt = P.add("vector",
                             lambda e, p=p: e.tensor_copy(PTb[p][:, 512:640],
                                                          pb[6][:, 0:128]),
                             deps=(ptev[4],), inc="ve")
            if sink:
                evpt = P.add("vector",
                             lambda e, p=p: e.tensor_copy(PTk[p][0:4, :],
                                                          pb[6][0:4, 128:256]),
                             deps=(ptev[-1],), inc="ve")
            if ns == 5 or sink:
                bf_ev[6] = evpt
            avev = None
            for kk in range(ns):
                sa = lo // 128 + kk
                avev = P.add("tensor",
                             lambda e, sa=sa, kk=kk, g=g, p=p, ns=ns, sink=sink:
                             e.matmul(
                                 pb[7][:, 0:128],
                                 v_sb[:, sa * 512 + g * 128:sa * 512 + g * 128 + 128],
                                 PTb[p][:, kk * 128:(kk + 1) * 128],
                                 start=(kk == 0), stop=(kk == ns - 1 and not sink)),
                             deps=(evpt, bf_ev[7], ev_v[sa]), inc="te")
            if sink:
                avev = P.add("tensor",
                             lambda e, g=g, p=p: e.matmul(
                                 pb[7][:, 0:128],
                                 v_sb[0:4, g * 128:g * 128 + 128],
                                 PTk[p][0:4, :], start=False, stop=True),
                             deps=(ev_v[0],), inc="te")
            evPT_free[p] = avev
            dd = (avev, ev_dumpq) if first_ao else (avev,)
            first_ao = False
            evao = P.add("vector",
                         lambda e, qs=qs: e.tensor_copy(
                             QT[:, qs[0]:qs[1]], pb[7][:, 0:128]),
                         deps=dd, inc="ve")
            bf_ev[7] = evao
            ev_ao[(h, qt)] = evao
            it += 1
    ev_C_done = ev_ao[(NH - 1, NT - 1)]

    if debug:
        P.dma("gpsimd", ao_h[:], QT[:], deps=(ev_C_done,))

    ev_bo = P.dma("gpsimd", bo_bf[:], bo_h[:], deps=(ev_qt[15], ev_kt[3]))
    ev_x1free = [(ev_x1sp[14],), (ev_x1sp[15],)]
    ev_x2sp = []
    ev_D_mm = None
    for tt in range(NT):
        b = tt % 2
        xb = (xb0, xb1)[b]
        evx = P.dma("sync", xb[:], x1_h[tt * 128:(tt + 1) * 128, :],
                    deps=(ev_x1sp[tt], ev_C_done) + ev_x1free[b])
        evacs = []
        for oc in range(4):
            k = 2 + oc % 2
            mm = None
            for hh in range(16):
                mm = P.add("tensor",
                           lambda e, hh=hh, oc=oc, tt=tt, k=k: e.matmul(
                               pb[k][:],
                               QT[:, hh * T + tt * 128:hh * T + tt * 128 + 128],
                               wo_all[:, hh * C + oc * 512:hh * C + oc * 512 + 512],
                               start=(hh == 0), stop=False),
                           deps=(ev_ao[(hh, tt)], ev_wo[hh], bf_ev[k]), inc="te")
            mm = P.add("tensor",
                       lambda e, k=k, oc=oc: e.matmul(
                           pb[k][:], ones_sb[:, 0:128], bo_bf[:, oc * 512:oc * 512 + 512],
                           start=False, stop=True),
                       deps=(ev_ones, ev_bo), inc="te")
            ev_D_mm = mm
            eve = P.add("vector",
                        lambda e, oc=oc, k=k, xb=xb: e.tensor_add(
                            xb[:, oc * 512:oc * 512 + 512], pb[k][:],
                            xb[:, oc * 512:oc * 512 + 512]),
                        deps=(mm, evx), inc="ve")
            bf_ev[k] = eve
            evacs.append(eve)
        evsp = P.dma("scalar", x2_h[tt * 128:(tt + 1) * 128, :], xb[:],
                     deps=(evacs[-1],))
        ev_x2sp.append(evsp)
        ev_x1free[b] = (evsp,)

    ev_bwe = []
    for i in range(16):
        dep = (ev_D_mm, ev_bwT_spill) if i == 0 else ()
        ev_bwe.append(P.dma("gpsimd", bwT_e[:, i * C:(i + 1) * C],
                            bwT_h[:, i * C:(i + 1) * C], deps=dep))
    ev_x2free = [(ev_x2sp[14],), (ev_x2sp[15],)]
    for tt in range(NT):
        b = tt % 2
        xb = (xb0, xb1)[b]
        evx = P.dma("sync", xb[:], x2_h[tt * 128:(tt + 1) * 128, :],
                    deps=(ev_x2sp[tt],) + ev_x2free[b])
        ev_ln3 = emit_ln(xb, ln_bf, evx, [ev_lnbf_free, ev_D_mm])
        ev_t3, t3mm = emit_transpose(ln_bf, lnT1_v, ev_ln3)
        ev_lnbf_free = t3mm
        evs = []
        for oc in range(4):
            k = 2 + oc % 2
            mm = None
            for i in range(16):
                mm = P.add("tensor",
                           lambda e, i=i, oc=oc, k=k: e.matmul(
                               pb[k][:], lnT1[:, i * 128:(i + 1) * 128],
                               bwT_e[:, i * C + oc * 512:i * C + oc * 512 + 512],
                               start=(i == 0), stop=False),
                           deps=(ev_t3, ev_bwe[i], bf_ev[k]), inc="te")
            mm = P.add("tensor",
                       lambda e, oc=oc, k=k: e.matmul(
                           pb[k][:], ones_sb[:, 0:128], bff_bf[:, oc * 512:oc * 512 + 512],
                           start=False, stop=True),
                       deps=(ev_ones, ev_bff), inc="te")
            eve = P.add("vector",
                        lambda e, oc=oc, k=k, xb=xb: e.tensor_add(
                            xb[:, oc * 512:oc * 512 + 512], pb[k][:],
                            xb[:, oc * 512:oc * 512 + 512]),
                        deps=(mm, evx), inc="ve")
            bf_ev[k] = eve
            evs.append(eve)
        evo = P.dma("scalar", out_h[tt * 128:(tt + 1) * 128, :], xb[:],
                    deps=(evs[-1],))
        ev_x2free[b] = (evo,)

    P.add("scalar", lambda e: e.nop(), deps=(("st", P.counts["st"]),))
    P.add("sync", lambda e: e.nop(), deps=(("ldx", P.counts["ldx"]),))
    P.add("gpsimd", lambda e: e.nop(), deps=(("ldw", P.counts["ldw"]),))

    with nc.Block() as block:
        P.emit(block)
    return nc


_NC_CACHE = {}


def _get_nc(debug=False):
    if debug not in _NC_CACHE:
        _NC_CACHE[debug] = build_nc(debug)
    return _NC_CACHE[debug]


def kernel(L0_attn_x, L0_attn_Wq, L0_attn_bq, L0_attn_Wk, L0_attn_bk,
           L0_attn_Wv, L0_attn_bv, L0_attn_Wo, L0_attn_bo,
           L0_ffn_weight, L0_ffn_bias, _debug=False):
    nc = _get_nc(_debug)
    x = np.ascontiguousarray(np.asarray(L0_attn_x), dtype=np.float32)
    wq = np.ascontiguousarray(np.asarray(L0_attn_Wq), dtype=np.float32)
    bq = np.ascontiguousarray(np.asarray(L0_attn_bq), dtype=np.float32).reshape(NH, 128)
    wk = np.ascontiguousarray(np.asarray(L0_attn_Wk), dtype=np.float32)
    bk = np.ascontiguousarray(np.asarray(L0_attn_bk), dtype=np.float32).reshape(NG, 128)
    wv = np.ascontiguousarray(np.asarray(L0_attn_Wv), dtype=np.float32)
    bv = np.ascontiguousarray(np.asarray(L0_attn_bv), dtype=np.float32).reshape(1, KV)
    wo = np.ascontiguousarray(np.asarray(L0_attn_Wo), dtype=np.float32)
    bo = np.ascontiguousarray(np.asarray(L0_attn_bo), dtype=np.float32).reshape(1, C)
    wf = np.ascontiguousarray(np.asarray(L0_ffn_weight), dtype=np.float32)
    bf = np.ascontiguousarray(np.asarray(L0_ffn_bias), dtype=np.float32).reshape(1, C)
    in_maps = []
    for b in range(8):
        in_maps.append({
            "x": np.ascontiguousarray(x[b]),
            "wq": wq, "bq": bq, "wk": wk, "bk": bk, "wv": wv, "bv": bv,
            "wo": wo, "bo": bo, "wf": wf, "bf": bf,
        })
    res = run_bass_kernel_spmd(nc, in_maps, core_ids=list(range(8)))
    if _debug:
        return res
    return np.stack([r["out"] for r in res.results], axis=0)
